# version 17
# baseline (speedup 1.0000x reference)
import contextlib
import numpy as np
import concourse.bass as bass
import concourse.mybir as mybir
from concourse.bass_utils import run_bass_kernel_spmd

F32 = mybir.dt.float32
BF16 = mybir.dt.bfloat16
AF = mybir.ActivationFunctionType
ALU = mybir.AluOpType
AX = mybir.AxisListType

T = 4096
D = 2048
H = 16
NW = 4
PREF = [3, 4, 7, 8]
MASKPOS = [0, 1, 4, 5]
NEG = -1.0e30
EPS = 1e-5

DEBUG = False


def build_program():
    nc = bass.Bass("TRN2", target_bir_lowering=False)

    x_h = nc.dram_tensor("x", [T, D], F32, kind="ExternalInput")
    wqk_h = nc.dram_tensor("wqk", [32, 128, 2048], F32, kind="ExternalInput")
    wv_h = nc.dram_tensor("wv", [4, 16, 128, 512], F32, kind="ExternalInput")
    wg_h = nc.dram_tensor("wg", [64, 128, 2048], F32, kind="ExternalInput")
    wu_h = nc.dram_tensor("wu", [64, 128, 2048], F32, kind="ExternalInput")
    wd_h = nc.dram_tensor("wd", [16, 128, 8192], F32, kind="ExternalInput")
    wo_h = nc.dram_tensor("wo", [16, 128, 2048], F32, kind="ExternalInput")
    bq_h = nc.dram_tensor("bq", [128, 48], F32, kind="ExternalInput")
    bo_h = nc.dram_tensor("bo", [128, 16], F32, kind="ExternalInput")
    mb_h = nc.dram_tensor("mb", [128, 16], F32, kind="ExternalInput")
    out_h = nc.dram_tensor("out", [2048, D], F32, kind="ExternalOutput")

    kts_h = nc.dram_tensor("kts", [16, 128, 4096], BF16, kind="Internal")
    vs_h = nc.dram_tensor("vs", [16, 4096, 128], BF16, kind="Internal")
    qts_h = nc.dram_tensor("qts", [16, 128, 2048], BF16, kind="Internal")
    xts_h = nc.dram_tensor("xts", [16, 128, 2048], F32, kind="Internal")
    wgb_h = nc.dram_tensor("wgb", [64, 128, 2048], BF16, kind="Internal")
    wub_h = nc.dram_tensor("wub", [64, 128, 2048], BF16, kind="Internal")
    wdb_h = nc.dram_tensor("wdb", [16, 128, 8192], BF16, kind="Internal")
    wob_h = nc.dram_tensor("wob", [16, 128, 2048], BF16, kind="Internal")

    dbg = {}
    if DEBUG:
        dbg["qts"] = nc.dram_tensor("dbg_qts", [16, 128, 2048], BF16,
                                    kind="ExternalOutput")
        dbg["kts"] = nc.dram_tensor("dbg_kts", [16, 128, 4096], BF16,
                                    kind="ExternalOutput")
        dbg["vs"] = nc.dram_tensor("dbg_vs", [16, 4096, 128], BF16,
                                   kind="ExternalOutput")
        dbg["y"] = nc.dram_tensor("dbg_y", [4, 128, 8192], BF16,
                                  kind="ExternalOutput")
        dbg["r1"] = nc.dram_tensor("dbg_r1", [4, 128, 8192], F32,
                                   kind="ExternalOutput")
        dbg["ln1"] = nc.dram_tensor("dbg_ln1", [4, 128, 8192], F32,
                                    kind="ExternalOutput")
        dbg["bias"] = nc.dram_tensor("dbg_bias", [4, 16, 128, 8], F32,
                                     kind="ExternalOutput")
        dbg["l"] = nc.dram_tensor("dbg_l", [4, 16, 512], F32,
                                  kind="ExternalOutput")

    AF32N = 17408
    af32 = nc.alloc_sbuf_tensor("af32", [128, AF32N], F32)

    def f32buf(c0, n):
        assert c0 + n <= AF32N
        return af32[:, c0:c0 + n]

    R1 = 0
    LN1 = 8192
    XTS = 16384
    SILU = 16896
    OSB = 0
    CVF = 12288

    ABFN = 62464
    abf = nc.alloc_sbuf_tensor("abf", [128, ABFN], BF16)

    def bfbuf(c0, n):
        assert c0 + n <= ABFN
        return abf[:, c0:c0 + n]

    ZONE = 0
    YW = 32768
    LN1T = 40960
    CVB = 43008
    WGB = 49152
    WUB = 53248
    WDB = 57344
    R1BF = 61440
    SQBF = 61952
    ZKT = 0
    ZV = 4096
    ZQT = 8192
    ZPT = 8704
    ZWO = 10752

    identf = nc.alloc_sbuf_tensor("identf", [128, 128], F32)
    ones_f = nc.alloc_sbuf_tensor("ones_f", [1, 128], F32)
    ones_cb = nc.alloc_sbuf_tensor("ones_cb", [128, 1], BF16)
    masktri = nc.alloc_sbuf_tensor("masktri", [128, 128], F32)
    bq_s = nc.alloc_sbuf_tensor("bq_s", [128, 48], F32)
    bo_s = nc.alloc_sbuf_tensor("bo_s", [128, 16], F32)
    bo2_s = nc.alloc_sbuf_tensor("bo2_s", [128, 16], F32)
    bv_bf = nc.alloc_sbuf_tensor("bv_bf", [128, 16], BF16)
    mb_s = nc.alloc_sbuf_tensor("mb_s", [128, 16], F32)
    bias_s = nc.alloc_sbuf_tensor("bias_s", [128, 8], F32)
    negm_s = nc.alloc_sbuf_tensor("negm_s", [128, 16], F32)
    nmrow = nc.alloc_sbuf_tensor("nmrow", [1, 64], F32)
    l_sb = nc.alloc_sbuf_tensor("l_sb", [1, 512], F32)
    rl_sb = nc.alloc_sbuf_tensor("rl_sb", [1, 512], F32)
    rlbc_sb = nc.alloc_sbuf_tensor("rlbc_sb", [128, 512], F32)
    eps_s = nc.alloc_sbuf_tensor("eps_s", [1, 1], F32)
    srow = nc.alloc_sbuf_tensor("srow", [1, 512], F32)
    qrow = nc.alloc_sbuf_tensor("qrow", [1, 512], F32)
    arow = nc.alloc_sbuf_tensor("arow", [1, 512], F32)
    brow = nc.alloc_sbuf_tensor("brow", [1, 512], F32)

    ps = nc.alloc_psum_tensor("ps", [128, 4096], F32)

    def bank(i, w=512):
        return ps[:, i * 512:i * 512 + w]

    sem_names = ["s_in", "s_gpd", "s_te", "s_dve", "s_act", "s_gpc",
                 "s_scd", "s_ka", "s_va", "s_qa", "s_wo", "s_ff", "s_xs",
                 "s_wv", "s_xa", "s_xp"]
    cnt = {s: 0 for s in sem_names}
    engs = ["sync", "tensor", "vector", "scalar", "gpsimd"]
    ops = {e: [] for e in engs}
    last_wait = {e: {s: 0 for s in sem_names} for e in engs}

    def emit(eng, fn, waits=(), inc=None, inc_by=1):
        real = []
        for s, v in waits:
            assert v is not None, (eng, s)
            if v > last_wait[eng][s]:
                real.append((s, v))
                last_wait[eng][s] = v
        ops[eng].append((real, fn, inc, inc_by))
        if inc is not None:
            cnt[inc] += inc_by
            return cnt[inc]
        return None

    def dma(eng, out, in_, waits=(), sem=None):
        if sem is None:
            sem = {"sync": "s_in", "scalar": "s_scd"}.get(eng, "s_gpd")
        return emit(eng, lambda e, o=out, i=in_: e.dma_start(out=o, in_=i),
                    waits=waits, inc=sem, inc_by=16)

    def gp_const(g):
        g.memset(identf[:], 1.0)
        g.affine_select(identf[:], identf[:], [[-1, 128]], ALU.is_equal, 0.0,
                        base=0, channel_multiplier=1)
        g.memset(ones_f[:], 1.0)
        g.memset(ones_cb[:], 1.0)
        g.memset(eps_s[:], EPS)
        g.memset(nmrow[:], 0.0)
        g.memset(masktri[:], 0.0)
        return g.affine_select(masktri[:], masktri[:], [[1, 128]], ALU.is_ge,
                               NEG, base=0, channel_multiplier=-1)

    e_const = emit("gpsimd", gp_const, inc="s_gpc")

    dma("sync", bq_s[:], bq_h.ap())
    dma("sync", bo_s[:], bo_h.ap())
    c_bias = dma("sync", mb_s[:], mb_h.ap())
    e_bv = emit("vector", lambda e: e.tensor_copy(bv_bf[:], bq_s[:, 32:48]),
                waits=[("s_in", c_bias)], inc="s_dve")

    cv_prev = [0, 0]
    conv_last = [0]

    def conv_stream(src_aps, dst_aps, gate=None):
        for i in range(len(src_aps)):
            buf = i % 2
            fsl = f32buf(CVF + buf * 2048, 2048)
            bsl = bfbuf(CVB + buf * 2048, 2048)
            ldw = []
            if i == 0 and gate is not None:
                ldw.append(gate)
            ld = dma("gpsimd", fsl, src_aps[i], waits=ldw)
            cvw = [("s_gpd", ld)]
            if cv_prev[buf]:
                cvw.append(("s_gpd", cv_prev[buf]))
            cv = emit("gpsimd",
                      lambda e, a=bsl, b=fsl: e.tensor_copy(a, b),
                      waits=cvw, inc="s_gpc")
            conv_last[0] = dma("gpsimd", dst_aps[i], bsl,
                               waits=[("s_gpc", cv)])
            cv_prev[buf] = conv_last[0]
        return conv_last[0]

    XL = [R1, R1 + 2048]
    XT32 = R1 + 4096
    WQA = [LN1, LN1 + 2048]
    WQBF = [YW, YW + 2048]
    KVE = LN1T

    e_tr = {}
    e_ev = {}
    e_cast = {}
    e_xspill = {}
    e_te_half = 0
    e_sp_q = 0
    e_sp_kv = 0
    e_sp_x = 0
    stage_bank_ev = {}
    stage_spill_ev = {}
    slab_f32_user = {0: 0, 1: 0}
    wq_last_mm = {}
    nsq_user = {}
    nm_ev = [0, 0]
    pend_norm = [None]
    norm_i = [0]
    NSQ = 47104

    def emit_norm(sq, ncol, nbuf):
        nm_ev[0] = emit("tensor",
                        lambda e, nbuf=nbuf:
                        e.matmul(ps[0:1, 2048:2560], ones_cb[:],
                                 bfbuf(nbuf, 512), start=True, stop=True),
                        waits=[("s_act", sq), ("s_dve", nm_ev[1])],
                        inc="s_te")
        rr = emit("vector",
                  lambda e: e.reduce_max(nmrow[0:1, 32:33],
                                         ps[0:1, 2048:2560], axis=AX.X),
                  waits=[("s_te", nm_ev[0])], inc="s_dve")
        nm_ev[1] = emit("vector",
                        lambda e, ncol=ncol:
                        e.tensor_max(nmrow[0:1, ncol:ncol + 1],
                                     nmrow[0:1, ncol:ncol + 1],
                                     nmrow[0:1, 32:33]),
                        waits=[("s_dve", rr)], inc="s_dve")

    for hf in range(2):
        for tt in range(16):
            gt = hf * 16 + tt
            xb = XL[gt % 2]
            ldw = []
            if gt - 2 in e_tr:
                ldw.append(("s_te", e_tr[gt - 2]))
            ld = dma("sync", f32buf(xb, 2048),
                     x_h.ap()[gt * 128:(gt + 1) * 128, :], waits=ldw,
                     sem="s_xa")
            b0 = 0 if gt % 2 == 0 else 4
            wt0 = [("s_xa", ld), ("s_gpc", e_const)]
            prev_dve = max([v for k, v in stage_bank_ev.items()
                            if k != "vmm"] + [nm_ev[1]], default=0)
            if prev_dve:
                wt0.append(("s_dve", prev_dve))
            te = None
            for c in range(16):
                te = emit("tensor",
                          lambda e, c=c, xb=xb, b0=b0:
                          e.transpose(ps[:, b0 * 512 + c * 128:
                                         b0 * 512 + c * 128 + 128],
                                      f32buf(xb + c * 128, 128), identf[:]),
                          waits=wt0 if c == 0 else [],
                          inc="s_te" if c == 15 else None)
            e_tr[gt] = te
            evw = [("s_te", te)]
            if gt - 1 in e_cast:
                evw.append(("s_act", e_cast[gt - 1]))
            if gt - 1 in e_xspill:
                evw.append(("s_xp", e_xspill[gt - 1]))
            ev = emit("vector",
                      lambda e, b0=b0: e.tensor_copy(
                          f32buf(XT32, 2048),
                          ps[:, b0 * 512:b0 * 512 + 2048]),
                      waits=evw, inc="s_dve")
            e_ev[gt] = ev
            stage_bank_ev[b0] = ev
            cw = [("s_dve", ev)]
            if e_te_half:
                cw.append(("s_te", e_te_half))
            cs = emit("scalar",
                      lambda e, tt=tt: e.activation(
                          abf[:, ZONE:ZONE + 32768].rearrange(
                              "p (c t) -> p c t", c=16)[:, :,
                                                        tt * 128:tt * 128
                                                        + 128],
                          f32buf(XT32, 2048).rearrange(
                              "p (c t) -> p c t", c=16),
                          AF.Copy, bias=0.0, scale=1.0),
                      waits=cw, inc="s_act")
            e_cast[gt] = cs
            if tt >= 8:
                own = hf * 8 + (tt - 8)
                e_xspill[gt] = dma("scalar", xts_h.ap()[own],
                                   f32buf(XT32, 2048),
                                   waits=[("s_dve", ev)], sem="s_xp")
                e_sp_x = e_xspill[gt]

        wt_x = [("s_act", e_cast[hf * 16 + 15])]

        for oc in range(32):
            isq = oc < 16
            h = oc % 16
            sl = oc % 2
            ldw = []
            if slab_f32_user[sl]:
                ldw.append(("s_act", slab_f32_user[sl]))
            ld = dma("sync", f32buf(WQA[sl], 2048), wqk_h.ap()[oc],
                     waits=ldw)
            wcw = [("s_in", ld)]
            if wq_last_mm.get(sl, 0):
                wcw.append(("s_te", wq_last_mm[sl]))
            if stage_bank_ev.get("vmm", 0):
                wcw.append(("s_te", stage_bank_ev["vmm"]))
            if oc == 0:
                wcw += wt_x
            wc = emit("scalar",
                      lambda e, sl=sl: e.activation(
                          bfbuf(WQBF[sl], 2048), f32buf(WQA[sl], 2048),
                          AF.Copy, bias=0.0, scale=1.0),
                      waits=wcw, inc="s_act")
            slab_f32_user[sl] = wc
            for tokb in range(4):
                pb = tokb % 2
                wt = [("s_act", wc)]
                bev = max(stage_bank_ev.get(("qk", pb), 0),
                          stage_bank_ev.get(0, 0))
                if bev:
                    wt.append(("s_dve", bev))
                te = None
                for c in range(16):
                    te = emit("tensor",
                              lambda e, c=c, sl=sl, tokb=tokb, pb=pb:
                              e.matmul(bank(pb),
                                       bfbuf(WQBF[sl] + c * 128, 128),
                                       abf[:, ZONE + c * 2048 + tokb * 512:
                                           ZONE + c * 2048 + tokb * 512
                                           + 512],
                                       start=(c == 0), stop=(c == 15)),
                              waits=wt if c == 0 else [],
                              inc="s_te" if c == 15 else None)
                wq_last_mm[sl] = te
                kve = KVE + pb * 512
                evw = [("s_te", te)]
                if stage_spill_ev.get(pb, 0):
                    evw.append(("s_scd", stage_spill_ev[pb]))
                if nsq_user.get(pb, 0):
                    evw.append(("s_act", nsq_user[pb]))
                if isq:
                    ev = emit("vector",
                              lambda e, kve=kve, pb=pb, oc=oc:
                              e.tensor_scalar(bfbuf(kve, 512), bank(pb),
                                              bq_s[:, oc:oc + 1],
                                              1.0 / 128.0,
                                              ALU.add, ALU.mult),
                              waits=evw, inc="s_dve")
                else:
                    ev = emit("vector",
                              lambda e, kve=kve, pb=pb, oc=oc:
                              e.tensor_scalar_add(bfbuf(kve, 512), bank(pb),
                                                  bq_s[:, oc:oc + 1]),
                              waits=evw, inc="s_dve")
                stage_bank_ev[("qk", pb)] = ev
                if isq:
                    if tokb >= 2:
                        slot = hf * 2 + (tokb - 2)
                        sp = dma("scalar",
                                 qts_h.ap()[h][:, slot * 512:slot * 512
                                               + 512],
                                 bfbuf(kve, 512), waits=[("s_dve", ev)])
                        stage_spill_ev[pb] = sp
                        e_sp_q = sp
                        do_norm = True
                    else:
                        do_norm = False
                else:
                    pos = hf * 4 + tokb
                    sp = dma("scalar",
                             kts_h.ap()[h][:, pos * 512:pos * 512 + 512],
                             bfbuf(kve, 512), waits=[("s_dve", ev)])
                    stage_spill_ev[pb] = sp
                    e_sp_kv = sp
                    do_norm = True
                if do_norm:
                    ncol = h if not isq else 16 + h
                    nbuf = NSQ + (norm_i[0] % 2) * 512
                    norm_i[0] += 1
                    sq = emit("scalar",
                              lambda e, kve=kve, nbuf=nbuf: e.activation(
                                  bfbuf(nbuf, 512), bfbuf(kve, 512),
                                  AF.Square, bias=0.0, scale=1.0),
                              waits=[("s_dve", ev), ("s_te", nm_ev[0])],
                              inc="s_act")
                    nsq_user[pb] = sq
                    emit_norm(sq, ncol, nbuf)

        for og in range(4):
            wv_cast = {}
            for c in range(16):
                sl = c % 2
                ldw = []
                if slab_f32_user[sl]:
                    ldw.append(("s_act", slab_f32_user[sl]))
                ld = dma("sync", f32buf(WQA[sl], 512), wv_h.ap()[og][c],
                         waits=ldw, sem="s_wv")
                wcw = [("s_wv", ld)]
                if stage_bank_ev.get("vmm", 0):
                    wcw.append(("s_te", stage_bank_ev["vmm"]))
                for mmv in wq_last_mm.values():
                    wcw.append(("s_te", mmv))
                wv_cast[c] = emit("scalar",
                                  lambda e, c=c, sl=sl: e.activation(
                                      bfbuf(YW + c * 512, 512),
                                      f32buf(WQA[sl], 512), AF.Copy,
                                      bias=0.0, scale=1.0),
                                  waits=wcw, inc="s_act")
                slab_f32_user[sl] = wv_cast[c]
            for tt in range(16):
                gt = hf * 16 + tt
                pb = 2 + tt % 2
                wt = [("s_act", wv_cast[15])]
                bev = max(stage_bank_ev.get(("v", pb), 0),
                          stage_bank_ev.get(0, 0))
                if bev:
                    wt.append(("s_dve", bev))
                te = None
                for c in range(16):
                    te = emit("tensor",
                              lambda e, c=c, tt=tt, pb=pb:
                              e.matmul(bank(pb),
                                       abf[:, ZONE + c * 2048 + tt * 128:
                                           ZONE + c * 2048 + tt * 128
                                           + 128],
                                       bfbuf(YW + c * 512, 512),
                                       start=(c == 0), stop=(c == 15)),
                              waits=wt if c == 0 else [],
                              inc="s_te" if c == 15 else None)
                stage_bank_ev["vmm"] = te
                e_te_half = te
                kve = KVE + 1024 + (tt % 2) * 512
                evw = [("s_te", te)]
                if stage_spill_ev.get(("v", tt % 2), 0):
                    evw.append(("s_scd", stage_spill_ev[("v", tt % 2)]))
                ev = emit("vector",
                          lambda e, kve=kve, pb=pb:
                          e.tensor_copy(bfbuf(kve, 512), bank(pb)),
                          waits=[w for w in evw], inc="s_dve")
                stage_bank_ev[("v", pb)] = ev
                sp = dma("scalar",
                         bass.AP(vs_h, og * 4 * T * 128 + gt * 128 * 128,
                                 [[128, 128], [T * 128, 4], [1, 128]]),
                         bfbuf(kve, 512).rearrange("p (h d) -> p h d", h=4),
                         waits=[("s_dve", ev)])
                stage_spill_ev[("v", tt % 2)] = sp
                e_sp_kv = sp

    if DEBUG:
        dbw = [("s_scd", e_sp_q), ("s_scd", e_sp_kv)]
        for hh in range(16):
            dma("gpsimd", dbg["qts"].ap()[hh], qts_h.ap()[hh], waits=dbw)
            dma("gpsimd", dbg["kts"].ap()[hh], kts_h.ap()[hh])
            dma("gpsimd", dbg["vs"].ap()[hh], vs_h.ap()[hh])

    if pend_norm[0] is not None:
        emit_norm(*pend_norm[0])
        pend_norm[0] = None

    e_conv_wo = conv_stream([wo_h.ap()[i] for i in range(16)],
                            [wob_h.ap()[i] for i in range(16)])
    e_conv_wg = conv_stream([wg_h.ap()[i] for i in range(64)],
                            [wgb_h.ap()[i] for i in range(64)],
                            gate=("s_te", e_te_half))
    e_conv_wu = conv_stream([wu_h.ap()[i] for i in range(64)],
                            [wub_h.ap()[i] for i in range(64)])
    e_conv_all = conv_stream(
        [wd_h.ap()[i // 4][:, (i % 4) * 2048:(i % 4) * 2048 + 2048]
         for i in range(64)],
        [wdb_h.ap()[i // 4][:, (i % 4) * 2048:(i % 4) * 2048 + 2048]
         for i in range(64)])

    e_nm1 = emit("vector",
                 lambda e: e.tensor_mul(nmrow[0:1, 32:48], nmrow[0:1, 0:16],
                                        nmrow[0:1, 16:32]),
                 waits=[("s_dve", nm_ev[1])], inc="s_dve")
    e_nm2 = emit("scalar",
                 lambda e: e.activation(nmrow[0:1, 48:64], nmrow[0:1, 32:48],
                                        AF.Sqrt, bias=0.0, scale=1.0),
                 waits=[("s_dve", e_nm1)], inc="s_act")
    e_nm3 = emit("vector",
                 lambda e: e.tensor_scalar_mul(nmrow[0:1, 32:48],
                                               nmrow[0:1, 48:64], -1.0),
                 waits=[("s_act", e_nm2)], inc="s_dve")
    e_nm4 = emit("tensor",
                 lambda e: e.matmul(ps[:, 2048:2064], ones_f[:],
                                    nmrow[0:1, 32:48], start=True, stop=True),
                 waits=[("s_dve", e_nm3), ("s_dve", nm_ev[1])], inc="s_te")
    e_negm = emit("vector",
                  lambda e: e.tensor_copy(negm_s[:], ps[:, 2048:2064]),
                  waits=[("s_te", e_nm4)], inc="s_dve")

    e_bo2mm = 0
    for dmc in range(16):
        ld = dma("sync", bfbuf(ZONE + ZWO, 2048), wob_h.ap()[dmc],
                 waits=[("s_gpd", e_conv_wo), ("s_te", e_te_half),
                        ("s_te", e_bo2mm)], sem="s_wo")
        for yc in range(16):
            ev = emit("tensor",
                      lambda e, yc=yc, dmc=dmc:
                      e.matmul(ps[:, 1024 + dmc:1024 + dmc + 1],
                               bfbuf(ZONE + ZWO + yc * 128, 128),
                               bv_bf[:, yc:yc + 1],
                               start=(yc == 0), stop=(yc == 15)),
                      waits=[("s_wo", ld), ("s_dve", e_bv)]
                      if yc == 0 else [],
                      inc="s_te" if yc == 15 else None)
            if ev is not None:
                e_bo2mm = ev
    e_bo2 = emit("vector",
                 lambda e: e.tensor_add(bo2_s[:], ps[:, 1024:1040], bo_s[:]),
                 waits=[("s_te", e_bo2mm)], inc="s_dve")

    e_out_last = 0
    zone_free_at = [("s_te", e_bo2mm)]
    prev_gu = [0]
    prev_dn = [0]
    prev_ln2 = [e_bo2]
    prev_exp_last = 0

    for w in range(NW):
        pref = PREF[w]
        nkt = pref * 4
        kw = pref * 512

        e_yev = {}
        e_attn = 0
        for h in range(H):
            ldqw = (list(zone_free_at) + [("s_scd", e_sp_kv),
                                          ("s_scd", e_sp_q),
                                          ("s_xp", e_sp_x)]) if h == 0 \
                else [("s_dve", e_yev[h - 1])]
            ldq = dma("sync", bfbuf(ZONE + ZQT, 512),
                      qts_h.ap()[h][:, w * 512:w * 512 + 512], waits=ldqw,
                      sem="s_qa")
            ldk = dma("sync", bfbuf(ZONE + ZKT, kw), kts_h.ap()[h][:, 0:kw],
                      sem="s_ka")
            ldv = dma("sync",
                      bfbuf(ZONE + ZV, nkt * 128).rearrange(
                          "p (t d) -> p t d", d=128),
                      bass.AP(vs_h, h * T * 128,
                              [[128, 128], [128 * 128, nkt], [1, 128]]),
                      sem="s_va")
            cmw = [("s_dve", e_negm), ("s_in", c_bias)]
            if prev_exp_last:
                cmw.append(("s_act", prev_exp_last))
            nmaskb = 2 if w < 2 else 4
            e_bias = 0
            for i in range(nmaskb):
                e_bias = emit("vector",
                              lambda e, i=i, w=w, h=h:
                              e.tensor_add(bias_s[:, 1 + i:2 + i],
                                           negm_s[:, h:h + 1],
                                           mb_s[:, w * 4 + i:w * 4 + i + 1]),
                              waits=cmw if i == 0 else [], inc="s_dve")

            STB = [4, 5, 2, 3]
            STD = 2
            pv_ev = 0
            exp_ev = {}
            pt_guard = {}
            for kt in range(nkt):
                blk = kt // 4
                sb = STB[kt % STD]
                wt = [("s_dve", e_bias), ("s_ka", ldk), ("s_qa", ldq)]
                if kt - STD in exp_ev:
                    wt.append(("s_act", exp_ev[kt - STD]))
                elif prev_exp_last:
                    wt.append(("s_act", prev_exp_last))
                elif sb in (2, 3):
                    wt.append(("s_dve", prev_ln2[0]))
                stt = emit("tensor",
                           lambda e, kt=kt, sb=sb:
                           e.matmul(bank(sb),
                                    bfbuf(ZONE + ZKT + kt * 128, 128),
                                    bfbuf(ZONE + ZQT, 512),
                                    start=True, stop=True),
                           waits=wt, inc="s_te")
                dve_ev = None
                if blk == pref - 1:
                    qs = kt % 4
                    dve_ev = emit("vector",
                                  lambda e, sb=sb, qs=qs:
                                  e.tensor_add(
                                      ps[:, sb * 512 + qs * 128:
                                         sb * 512 + qs * 128 + 128],
                                      ps[:, sb * 512 + qs * 128:
                                         sb * 512 + qs * 128 + 128],
                                      masktri[:]),
                                  waits=[("s_te", stt)], inc="s_dve")
                if blk in MASKPOS and blk < pref - 1:
                    bias_ap = bias_s[:, 1 + MASKPOS.index(blk):
                                     2 + MASKPOS.index(blk)]
                else:
                    bias_ap = negm_s[:, h:h + 1]
                ptb = ZONE + ZPT + (kt % STD) * 512
                ew = [("s_te", stt)]
                if dve_ev:
                    ew.append(("s_dve", dve_ev))
                if kt - STD in pt_guard:
                    ew.append(("s_te", pt_guard[kt - STD]))
                exp_ev[kt] = emit("scalar",
                                  lambda e, sb=sb, ptb=ptb, b=bias_ap:
                                  e.activation(bfbuf(ptb, 512), bank(sb),
                                               AF.Exp, bias=b, scale=1.0),
                                  waits=ew, inc="s_act")
                pt_ready = None
                if blk == pref - 1:
                    ktr = kt % 4
                    for qs in range(4):
                        if ktr > qs:
                            pt_ready = emit(
                                "vector",
                                lambda e, ptb=ptb, qs=qs:
                                e.memset(bfbuf(ptb + qs * 128, 128), 0.0),
                                waits=[("s_act", exp_ev[kt])], inc="s_dve")
                pvw = [("s_act", exp_ev[kt]), ("s_va", ldv)]
                if pt_ready is not None:
                    pvw.append(("s_dve", pt_ready))
                if kt == 0 and h > 0:
                    pvw.append(("s_dve", e_yev[h - 1]))
                pv_ev = emit("tensor",
                             lambda e, kt=kt, ptb=ptb, nkt=nkt:
                             e.matmul(bank(6),
                                      abf[:, ZONE + ZV + kt * 128:
                                          ZONE + ZV + kt * 128 + 128],
                                      bfbuf(ptb, 512),
                                      start=(kt == 0), stop=(kt == nkt - 1)),
                             waits=pvw, inc="s_te")
                pt_guard[kt] = emit(
                    "tensor",
                    lambda e, kt=kt, ptb=ptb, nkt=nkt:
                    e.matmul(ps[0:1, 3584:4096], ones_cb[:], bfbuf(ptb, 512),
                             start=(kt == 0), stop=(kt == nkt - 1)),
                    inc="s_te")
            el = emit("vector",
                      lambda e: e.tensor_copy(l_sb[:], ps[0:1, 3584:4096]),
                      waits=[("s_te", pt_guard[nkt - 1])], inc="s_dve")
            er = emit("vector", lambda e: e.reciprocal(rl_sb[:], l_sb[:]),
                      waits=[("s_dve", el)], inc="s_dve")
            eb = emit("tensor",
                      lambda e: e.matmul(ps[:, 0:512], ones_f[:], rl_sb[:],
                                         start=True, stop=True),
                      waits=[("s_dve", er)], inc="s_te")
            ec = emit("vector",
                      lambda e: e.tensor_copy(rlbc_sb[:], ps[:, 0:512]),
                      waits=[("s_te", eb)], inc="s_dve")
            e_yev[h] = emit("vector",
                            lambda e, h=h:
                            e.tensor_mul(bfbuf(YW + h * 512, 512), bank(6),
                                         rlbc_sb[:]),
                            waits=[("s_te", pv_ev), ("s_dve", ec)],
                            inc="s_dve")
            prev_exp_last = exp_ev[nkt - 1]
            if DEBUG:
                dma("gpsimd", dbg["bias"].ap()[w][h], bias_s[:],
                    waits=[("s_dve", e_bias)])
                dma("gpsimd", dbg["l"].ap()[w][h:h + 1, :], l_sb[:],
                    waits=[("s_dve", e_yev[h])])
        e_attn = e_yev[H - 1]
        if DEBUG:
            dma("gpsimd", dbg["y"].ap()[w], bfbuf(YW, 8192),
                waits=[("s_dve", e_attn)])

        e_r1 = {}
        op_mm = {}
        for dmc in range(16):
            wsl = ZONE + ZWO + (dmc % 2) * 2048
            ldw = list(zone_free_at)
            if dmc - 2 in op_mm:
                ldw = [("s_te", op_mm[dmc - 2])]
            ldwv = dma("sync", bfbuf(wsl, 2048), wob_h.ap()[dmc],
                       waits=ldw, sem="s_wo")
            ldxw = [("s_xp", e_sp_x)]
            if dmc - 1 in e_r1:
                ldxw.append(("s_dve", e_r1[dmc - 1]))
            ldx = dma("sync",
                      f32buf(XTS, 512).rearrange("p (a b) -> p a b", a=4),
                      bass.AP(xts_h, w * 4 * 128 * 2048 + dmc * 128,
                              [[2048, 128], [128 * 2048, 4], [1, 128]]),
                      waits=ldxw, sem="s_xs")
            pb = dmc % 2
            wt0 = [("s_wo", ldwv), ("s_dve", e_attn)]
            if dmc - 2 in e_r1:
                wt0.append(("s_dve", e_r1[dmc - 2]))
            for yc in range(16):
                ev = emit("tensor",
                          lambda e, yc=yc, pb=pb, wsl=wsl:
                          e.matmul(bank(pb),
                                   bfbuf(wsl + yc * 128, 128),
                                   bfbuf(YW + yc * 512, 512),
                                   start=(yc == 0), stop=(yc == 15)),
                          waits=wt0 if yc == 0 else [],
                          inc="s_te" if yc == 15 else None)
                if ev is not None:
                    op_mm[dmc] = ev
            r1w = [("s_te", op_mm[dmc]), ("s_xs", ldx), ("s_dve", e_bo2)]
            if dmc < 4:
                r1w.append(("s_gpd", e_out_last))
            e_r1[dmc] = emit(
                "vector",
                lambda e, dmc=dmc, pb=pb:
                e.scalar_tensor_tensor(f32buf(R1 + dmc * 512, 512), bank(pb),
                                       bo2_s[:, dmc:dmc + 1],
                                       f32buf(XTS, 512), ALU.add, ALU.add),
                waits=r1w, inc="s_dve")
        if DEBUG:
            dma("gpsimd", dbg["r1"].ap()[w], f32buf(R1, 8192),
                waits=[("s_dve", e_r1[15])])

        def layer_norm(src_c0, dst_c0, dstbf_c0, wait_ev):
            mm_s = mm_q = 0
            for dmc in range(16):
                c1w = [("s_dve", wait_ev[dmc])]
                if mm_s:
                    c1w.append(("s_te", mm_s))
                c1 = emit("scalar",
                          lambda e, dmc=dmc:
                          e.activation(bfbuf(R1BF, 512),
                                       f32buf(src_c0 + dmc * 512, 512),
                                       AF.Copy, bias=0.0, scale=1.0),
                          waits=c1w, inc="s_act")
                c2w = []
                if mm_q:
                    c2w.append(("s_te", mm_q))
                c2 = emit("scalar",
                          lambda e, dmc=dmc:
                          e.activation(bfbuf(SQBF, 512),
                                       f32buf(src_c0 + dmc * 512, 512),
                                       AF.Square, bias=0.0, scale=1.0),
                          waits=c2w, inc="s_act")
                mm_s = emit("tensor",
                            lambda e, dmc=dmc:
                            e.matmul(ps[0:1, 1024:1536], ones_cb[:],
                                     bfbuf(R1BF, 512),
                                     start=(dmc == 0), stop=(dmc == 15)),
                            waits=[("s_act", c1)], inc="s_te")
                mm_q = emit("tensor",
                            lambda e, dmc=dmc:
                            e.matmul(ps[0:1, 1536:2048], ones_cb[:],
                                     bfbuf(SQBF, 512),
                                     start=(dmc == 0), stop=(dmc == 15)),
                            waits=[("s_act", c2)], inc="s_te")
            v1 = emit("vector",
                      lambda e: e.tensor_scalar_mul(srow[:],
                                                    ps[0:1, 1024:1536],
                                                    1.0 / 2048.0),
                      waits=[("s_te", mm_s), ("s_te", mm_q)], inc="s_dve")
            emit("vector",
                 lambda e: e.tensor_scalar_mul(qrow[:], ps[0:1, 1536:2048],
                                               1.0 / 2048.0),
                 inc="s_dve")
            v2 = emit("vector",
                      lambda e: e.tensor_mul(arow[:], srow[:], srow[:]),
                      waits=[("s_dve", v1)], inc="s_dve")
            ev = emit("vector",
                      lambda e: e.tensor_sub(qrow[:], qrow[:], arow[:]),
                      waits=[("s_dve", v2)], inc="s_dve")
            ev = emit("scalar",
                      lambda e: e.activation(arow[:], qrow[:], AF.Sqrt,
                                             bias=eps_s[:], scale=1.0),
                      waits=[("s_dve", ev)], inc="s_act")
            v3 = emit("vector", lambda e: e.reciprocal(arow[:], arow[:]),
                      waits=[("s_act", ev)], inc="s_dve")
            ev = emit("vector",
                      lambda e: e.tensor_mul(brow[:], srow[:], arow[:]),
                      waits=[("s_dve", v3)], inc="s_dve")
            emit("tensor",
                 lambda e: e.matmul(ps[:, 1024:1536], ones_f[:], arow[:],
                                    start=True, stop=True),
                 waits=[("s_dve", ev)], inc="s_te")
            eB = emit("tensor",
                      lambda e: e.matmul(ps[:, 1536:2048], ones_f[:],
                                         brow[:], start=True, stop=True),
                      inc="s_te")
            out_ev = {}
            for dmc in range(16):
                e1 = emit("vector",
                          lambda e, dmc=dmc:
                          e.tensor_mul(f32buf(SILU, 512),
                                       f32buf(src_c0 + dmc * 512, 512),
                                       ps[:, 1024:1536]),
                          waits=[("s_te", eB)] if dmc == 0 else [],
                          inc="s_dve")
                e2 = emit("vector",
                          lambda e, dmc=dmc:
                          e.tensor_sub(f32buf(dst_c0 + dmc * 512, 512),
                                       f32buf(SILU, 512),
                                       ps[:, 1536:2048]),
                          waits=[("s_dve", e1), ("s_gpd", e_conv_all)],
                          inc="s_dve")
                if dstbf_c0 is not None:
                    e2 = emit("scalar",
                              lambda e, dmc=dmc:
                              e.activation(bfbuf(dstbf_c0 + dmc * 512, 512),
                                           f32buf(dst_c0 + dmc * 512, 512),
                                           AF.Copy, bias=0.0, scale=1.0),
                              waits=[("s_dve", e2),
                                     ("s_gpd", e_conv_all)], inc="s_act")
                out_ev[dmc] = e2
            return out_ev

        ln1_ev = layer_norm(R1, LN1, LN1T, e_r1)
        if DEBUG:
            dma("gpsimd", dbg["ln1"].ap()[w], f32buf(LN1, 8192),
                waits=[("s_act", ln1_ev[15])])

        gu_mm = {}
        last_sil = {}
        last_mul = {}
        for hc in range(64):
            gsl = WGB + (hc % 2) * 2048
            usl = WUB + (hc % 2) * 2048
            if hc - 2 in gu_mm:
                ldw = [("s_te", gu_mm[hc - 2])]
            else:
                ldw = [("s_te", prev_gu[0]), ("s_gpd", e_conv_wu)]
            lg = dma("sync", bfbuf(gsl, 2048), wgb_h.ap()[hc], waits=ldw,
                     sem="s_ff")
            lu = dma("sync", bfbuf(usl, 2048), wub_h.ap()[hc], sem="s_ff")
            pb = (hc % 2) * 2
            wt0 = [("s_ff", lu), ("s_act", ln1_ev[15])]
            if hc - 2 in last_sil:
                wt0.append(("s_act", last_sil[hc - 2]))
                wt0.append(("s_dve", last_mul[hc - 2]))
            for c in range(16):
                emit("tensor",
                     lambda e, c=c, pb=pb, gsl=gsl:
                     e.matmul(bank(pb), bfbuf(gsl + c * 128, 128),
                              bfbuf(LN1T + c * 512, 512),
                              start=(c == 0), stop=(c == 15)),
                     waits=wt0 if c == 0 else [])
            for c in range(16):
                ev = emit("tensor",
                          lambda e, c=c, pb=pb, usl=usl:
                          e.matmul(bank(pb + 1), bfbuf(usl + c * 128, 128),
                                   bfbuf(LN1T + c * 512, 512),
                                   start=(c == 0), stop=(c == 15)),
                          inc="s_te" if c == 15 else None)
                if ev is not None:
                    gu_mm[hc] = ev
            slw = [("s_te", gu_mm[hc])]
            if hc - 1 in last_mul:
                slw.append(("s_dve", last_mul[hc - 1]))
            sl = emit("scalar",
                      lambda e, pb=pb:
                      e.activation(f32buf(SILU, 512), bank(pb), AF.Silu,
                                   bias=0.0, scale=1.0),
                      waits=slw, inc="s_act")
            last_sil[hc] = sl
            last_mul[hc] = emit("vector",
                                lambda e, hc=hc, pb=pb:
                                e.tensor_mul(bfbuf(ZONE + hc * 512, 512),
                                             f32buf(SILU, 512),
                                             bank(pb + 1)),
                                waits=[("s_act", sl)], inc="s_dve")
        e_r2 = {}
        dn_mm = {}
        for dmc in range(16):
            pb = 6 + dmc % 2
            for q in range(4):
                qi = dmc * 4 + q
                dsl = WDB + (qi % 2) * 2048
                if qi - 2 in dn_mm:
                    ldw = [("s_te", dn_mm[qi - 2])]
                else:
                    ldw = [("s_te", prev_dn[0]), ("s_gpd", e_conv_all)]
                ldd = dma("sync", bfbuf(dsl, 2048),
                          wdb_h.ap()[dmc][:, q * 2048:q * 2048 + 2048],
                          waits=ldw, sem="s_ff")
                wt0 = [("s_ff", ldd), ("s_dve", last_mul[63])]
                if q == 0 and dmc - 2 in e_r2:
                    wt0.append(("s_dve", e_r2[dmc - 2]))
                for hh in range(16):
                    hc = q * 16 + hh
                    ev = emit("tensor",
                              lambda e, hh=hh, hc=hc, pb=pb, dsl=dsl:
                              e.matmul(bank(pb), bfbuf(dsl + hh * 128, 128),
                                       bfbuf(ZONE + hc * 512, 512),
                                       start=(hc == 0), stop=(hc == 63)),
                              waits=wt0 if hh == 0 else [],
                              inc="s_te" if hh == 15 else None)
                    if ev is not None:
                        dn_mm[qi] = ev
            e_r2[dmc] = emit("vector",
                             lambda e, dmc=dmc, pb=pb:
                             e.tensor_add(f32buf(R1 + dmc * 512, 512),
                                          bank(pb),
                                          f32buf(LN1 + dmc * 512, 512)),
                             waits=[("s_te", dn_mm[dmc * 4 + 3])],
                             inc="s_dve")

        ln2_ev = layer_norm(R1, LN1, None, e_r2)

        e_osb = {}
        tte = 0
        for ts in range(4):
            for dmc in range(16):
                wt0 = [("s_dve", ln2_ev[15])]
                if dmc == 0 and ts - 1 in e_osb:
                    wt0.append(("s_dve", e_osb[ts - 1]))
                ev = emit("tensor",
                          lambda e, dmc=dmc, ts=ts:
                          e.transpose(ps[:, dmc * 128:dmc * 128 + 128],
                                      f32buf(LN1 + dmc * 512 + ts * 128,
                                             128),
                                      identf[:]),
                          waits=wt0, inc="s_te" if dmc == 15 else None)
                if ev is not None:
                    tte = ev
            e_osb[ts] = emit("vector",
                             lambda e: e.tensor_copy(f32buf(OSB, 2048),
                                                     ps[:, 0:2048]),
                             waits=[("s_te", tte), ("s_gpd", e_out_last)],
                             inc="s_dve")
            e_out_last = dma("gpsimd",
                             out_h.ap()[w * 512 + ts * 128:
                                        w * 512 + ts * 128 + 128, :],
                             f32buf(OSB, 2048),
                             waits=[("s_dve", e_osb[ts])])
        zone_free_at = [("s_te", dn_mm[63]), ("s_te", tte)]
        prev_gu[0] = gu_mm[63]
        prev_dn[0] = dn_mm[63]
        prev_ln2[0] = ln2_ev[15]

    emit("gpsimd", lambda e: None, waits=[("s_gpd", e_out_last)])

    with contextlib.ExitStack() as stack:
        sems = {s: stack.enter_context(nc.semaphore(s)) for s in sem_names}
        block = stack.enter_context(nc.Block())

        def make_body(eng_name):
            def body(e):
                for waits, fn, inc, inc_by in ops[eng_name]:
                    for s, v in waits:
                        e.wait_ge(sems[s], v)
                    inst = fn(e)
                    if inc is not None:
                        inst.then_inc(sems[inc], inc_by)
            return body

        block.sync(make_body("sync"))
        block.tensor(make_body("tensor"))
        block.vector(make_body("vector"))
        block.scalar(make_body("scalar"))
        block.gpsimd(make_body("gpsimd"))
    return nc


def _host_pack(W_qkv, w_gate, w_up, w_down, W_out):
    W = np.asarray(W_qkv, dtype=np.float32)
    wqk = np.empty((32, 128, 2048), dtype=np.float32)
    for oc in range(32):
        blk = W[:, oc * 128:(oc + 1) * 128]
        wqk[oc] = blk.reshape(16, 128, 128).transpose(1, 0, 2).reshape(
            128, 2048)
    wv = np.empty((4, 16, 128, 512), dtype=np.float32)
    for og in range(4):
        wv[og] = W[:, 4096 + og * 512:4096 + (og + 1) * 512].reshape(
            16, 128, 512)
    G = np.asarray(w_gate, dtype=np.float32).T
    U = np.asarray(w_up, dtype=np.float32).T
    Dn = np.asarray(w_down, dtype=np.float32).T
    Wo = np.asarray(W_out, dtype=np.float32)

    def pack(M, n_out):
        din = M.shape[0]
        ncin = din // 128
        out = np.empty((n_out, 128, ncin * 128), dtype=np.float32)
        for o in range(n_out):
            blk = M[:, o * 128:(o + 1) * 128]
            out[o] = blk.reshape(ncin, 128, 128).transpose(1, 0, 2).reshape(
                128, ncin * 128)
        return out

    return (wqk, wv, pack(G, 64), pack(U, 64), pack(Dn, 16), pack(Wo, 16))


PERM = {0: [2, 3, 0, 1, 4, 5, 6, 7],
        1: [0, 1, 2, 3, 6, 7, 4, 5]}
OWN = {0: [0, 1, 6, 7], 1: [2, 3, 4, 5]}
MB = {0: [[NEG, NEG, 0.0, 0.0], [NEG, NEG, 0.0, 0.0],
          [0.0, 0.0, 0.0, 0.0], [0.0, 0.0, 0.0, 0.0]],
      1: [[0.0, 0.0, 0.0, 0.0], [0.0, 0.0, 0.0, 0.0],
          [0.0, 0.0, NEG, NEG], [0.0, 0.0, NEG, NEG]]}

_cache = {}


def kernel(L0_attn_x, L0_attn_W_qkv, L0_attn_b_qkv, L0_attn_W_out,
           L0_attn_b_out, L0_ffn_w_gate, L0_ffn_w_up, L0_ffn_w_down):
    x = np.ascontiguousarray(np.asarray(L0_attn_x, dtype=np.float32))
    B = x.shape[0]
    wqk, wv, wg, wu, wd, wo = _host_pack(L0_attn_W_qkv, L0_ffn_w_gate,
                                         L0_ffn_w_up, L0_ffn_w_down,
                                         L0_attn_W_out)
    bq = np.ascontiguousarray(
        np.asarray(L0_attn_b_qkv, dtype=np.float32).reshape(48, 128).T)
    bo = np.ascontiguousarray(
        np.asarray(L0_attn_b_out, dtype=np.float32).reshape(16, 128).T)

    if "nc" not in _cache:
        _cache["nc"] = build_program()
    nc = _cache["nc"]

    in_maps = []
    for core in range(8):
        b, half = core // 2, core % 2
        xb = np.ascontiguousarray(
            x[b].reshape(8, 512, 2048)[PERM[half]].reshape(4096, 2048))
        mb = np.zeros((128, 16), dtype=np.float32)
        for s in range(4):
            for i in range(4):
                mb[:, s * 4 + i] = MB[half][s][i]
        in_maps.append({"x": xb, "wqk": wqk, "wv": wv, "wg": wg, "wu": wu,
                        "wd": wd, "wo": wo, "bq": bq, "bo": bo, "mb": mb})

    res = run_bass_kernel_spmd(nc, in_maps, core_ids=list(range(8)))

    out = np.empty((B, T, D), dtype=np.float32)
    for core in range(8):
        b, half = core // 2, core % 2
        shard = res.results[core]["out"]
        for s, g in enumerate(OWN[half]):
            out[b, g * 512:(g + 1) * 512] = shard[s * 512:(s + 1) * 512]
    if DEBUG:
        kernel.last_results = res
    return out


# version 20
# speedup vs baseline: 1.2414x; 1.2414x over previous
import contextlib
import numpy as np
import concourse.bass as bass
import concourse.mybir as mybir
from concourse.bass_utils import run_bass_kernel_spmd

F32 = mybir.dt.float32
BF16 = mybir.dt.bfloat16
AF = mybir.ActivationFunctionType
ALU = mybir.AluOpType
AX = mybir.AxisListType

T = 4096
D = 2048
H = 16
NW = 4
PREF = [3, 4, 7, 8]
MASKPOS = [0, 1, 4, 5]
NEG = -1.0e30
EPS = 1e-5

DEBUG = False


def build_program():
    nc = bass.Bass("TRN2", target_bir_lowering=False)

    x_h = nc.dram_tensor("x", [T, D], F32, kind="ExternalInput")
    wqk_h = nc.dram_tensor("wqk", [32, 128, 2048], F32, kind="ExternalInput")
    wv_h = nc.dram_tensor("wv", [4, 16, 128, 512], F32, kind="ExternalInput")
    wg_h = nc.dram_tensor("wg", [64, 128, 2048], F32, kind="ExternalInput")
    wu_h = nc.dram_tensor("wu", [64, 128, 2048], F32, kind="ExternalInput")
    wd_h = nc.dram_tensor("wd", [16, 128, 8192], F32, kind="ExternalInput")
    wo_h = nc.dram_tensor("wo", [16, 128, 2048], F32, kind="ExternalInput")
    bq_h = nc.dram_tensor("bq", [128, 48], F32, kind="ExternalInput")
    bo_h = nc.dram_tensor("bo", [128, 16], F32, kind="ExternalInput")
    mb_h = nc.dram_tensor("mb", [128, 16], F32, kind="ExternalInput")
    out_h = nc.dram_tensor("out", [2048, D], F32, kind="ExternalOutput")

    kts_h = nc.dram_tensor("kts", [16, 128, 4096], BF16, kind="Internal")
    vs_h = nc.dram_tensor("vs", [16, 4096, 128], BF16, kind="Internal")
    qts_h = nc.dram_tensor("qts", [16, 128, 2048], BF16, kind="Internal")
    xts_h = nc.dram_tensor("xts", [16, 128, 2048], F32, kind="Internal")
    wgb_h = nc.dram_tensor("wgb", [64, 128, 2048], BF16, kind="Internal")
    wub_h = nc.dram_tensor("wub", [64, 128, 2048], BF16, kind="Internal")
    wdb_h = nc.dram_tensor("wdb", [16, 128, 8192], BF16, kind="Internal")
    wob_h = nc.dram_tensor("wob", [16, 128, 2048], BF16, kind="Internal")

    dbg = {}
    if DEBUG:
        dbg["qts"] = nc.dram_tensor("dbg_qts", [16, 128, 2048], BF16,
                                    kind="ExternalOutput")
        dbg["kts"] = nc.dram_tensor("dbg_kts", [16, 128, 4096], BF16,
                                    kind="ExternalOutput")
        dbg["vs"] = nc.dram_tensor("dbg_vs", [16, 4096, 128], BF16,
                                   kind="ExternalOutput")
        dbg["y"] = nc.dram_tensor("dbg_y", [4, 128, 8192], BF16,
                                  kind="ExternalOutput")
        dbg["r1"] = nc.dram_tensor("dbg_r1", [4, 128, 8192], F32,
                                   kind="ExternalOutput")
        dbg["ln1"] = nc.dram_tensor("dbg_ln1", [4, 128, 8192], F32,
                                    kind="ExternalOutput")
        dbg["bias"] = nc.dram_tensor("dbg_bias", [4, 16, 128, 8], F32,
                                     kind="ExternalOutput")
        dbg["l"] = nc.dram_tensor("dbg_l", [4, 16, 512], F32,
                                  kind="ExternalOutput")

    AF32N = 17408
    af32 = nc.alloc_sbuf_tensor("af32", [128, AF32N], F32)

    def f32buf(c0, n):
        assert c0 + n <= AF32N
        return af32[:, c0:c0 + n]

    R1 = 0
    LN1 = 8192
    XTS = 16384
    SILU = 16896
    OSB = 0
    CVF = 12288

    ABFN = 62464
    abf = nc.alloc_sbuf_tensor("abf", [128, ABFN], BF16)

    def bfbuf(c0, n):
        assert c0 + n <= ABFN
        return abf[:, c0:c0 + n]

    ZONE = 0
    YW = 32768
    LN1T = 40960
    CVB = 43008
    WGB = 49152
    WUB = 53248
    WDB = 57344
    R1BF = 61440
    SQBF = 61952
    ZKT = 0
    ZV = 8192
    ZQT = 16384
    ZPT = 17408
    ZWO = 19456

    identf = nc.alloc_sbuf_tensor("identf", [128, 128], F32)
    ones_f = nc.alloc_sbuf_tensor("ones_f", [1, 128], F32)
    ones_cb = nc.alloc_sbuf_tensor("ones_cb", [128, 1], BF16)
    masktri = nc.alloc_sbuf_tensor("masktri", [128, 128], F32)
    bq_s = nc.alloc_sbuf_tensor("bq_s", [128, 48], F32)
    bo_s = nc.alloc_sbuf_tensor("bo_s", [128, 16], F32)
    bo2_s = nc.alloc_sbuf_tensor("bo2_s", [128, 16], F32)
    bv_bf = nc.alloc_sbuf_tensor("bv_bf", [128, 16], BF16)
    mb_s = nc.alloc_sbuf_tensor("mb_s", [128, 16], F32)
    bias_s = nc.alloc_sbuf_tensor("bias_s", [128, 8], F32)
    negm_s = nc.alloc_sbuf_tensor("negm_s", [128, 16], F32)
    nmrow = nc.alloc_sbuf_tensor("nmrow", [1, 64], F32)
    l_sb = nc.alloc_sbuf_tensor("l_sb", [1, 512], F32)
    rl_sb = nc.alloc_sbuf_tensor("rl_sb", [1, 512], F32)
    rlbc_sb = nc.alloc_sbuf_tensor("rlbc_sb", [128, 512], F32)
    eps_s = nc.alloc_sbuf_tensor("eps_s", [1, 1], F32)
    srow = nc.alloc_sbuf_tensor("srow", [1, 512], F32)
    qrow = nc.alloc_sbuf_tensor("qrow", [1, 512], F32)
    arow = nc.alloc_sbuf_tensor("arow", [1, 512], F32)
    brow = nc.alloc_sbuf_tensor("brow", [1, 512], F32)

    ps = nc.alloc_psum_tensor("ps", [128, 4096], F32)

    def bank(i, w=512):
        return ps[:, i * 512:i * 512 + w]

    sem_names = ["s_in", "s_gpd", "s_te", "s_dve", "s_act", "s_gpc",
                 "s_scd", "s_ka", "s_va", "s_qa", "s_wo", "s_ff", "s_xs",
                 "s_wv", "s_xa", "s_xp", "s_fu", "s_fd"]
    cnt = {s: 0 for s in sem_names}
    engs = ["sync", "tensor", "vector", "scalar", "gpsimd"]
    ops = {e: [] for e in engs}
    last_wait = {e: {s: 0 for s in sem_names} for e in engs}

    def emit(eng, fn, waits=(), inc=None, inc_by=1):
        real = []
        for s, v in waits:
            assert v is not None, (eng, s)
            if v > last_wait[eng][s]:
                real.append((s, v))
                last_wait[eng][s] = v
        ops[eng].append((real, fn, inc, inc_by))
        if inc is not None:
            cnt[inc] += inc_by
            return cnt[inc]
        return None

    def dma(eng, out, in_, waits=(), sem=None):
        if sem is None:
            sem = {"sync": "s_in", "scalar": "s_scd"}.get(eng, "s_gpd")
        return emit(eng, lambda e, o=out, i=in_: e.dma_start(out=o, in_=i),
                    waits=waits, inc=sem, inc_by=16)

    def gp_const(g):
        g.memset(identf[:], 1.0)
        g.affine_select(identf[:], identf[:], [[-1, 128]], ALU.is_equal, 0.0,
                        base=0, channel_multiplier=1)
        g.memset(ones_f[:], 1.0)
        g.memset(ones_cb[:], 1.0)
        g.memset(eps_s[:], EPS)
        g.memset(nmrow[:], 0.0)
        g.memset(masktri[:], 0.0)
        return g.affine_select(masktri[:], masktri[:], [[1, 128]], ALU.is_ge,
                               NEG, base=0, channel_multiplier=-1)

    e_const = emit("gpsimd", gp_const, inc="s_gpc")

    dma("sync", bq_s[:], bq_h.ap())
    dma("sync", bo_s[:], bo_h.ap())
    c_bias = dma("sync", mb_s[:], mb_h.ap())
    e_bv = emit("vector", lambda e: e.tensor_copy(bv_bf[:], bq_s[:, 32:48]),
                waits=[("s_in", c_bias)], inc="s_dve")

    cv_prev = [0, 0]
    conv_last = [0]

    def conv_stream(src_aps, dst_aps, gate=None):
        for i in range(len(src_aps)):
            buf = i % 2
            fsl = f32buf(CVF + buf * 2048, 2048)
            bsl = bfbuf(CVB + buf * 2048, 2048)
            ldw = []
            if i == 0 and gate is not None:
                ldw.append(gate)
            ld = dma("gpsimd", fsl, src_aps[i], waits=ldw)
            cvw = [("s_gpd", ld)]
            if cv_prev[buf]:
                cvw.append(("s_gpd", cv_prev[buf]))
            cv = emit("gpsimd",
                      lambda e, a=bsl, b=fsl: e.tensor_copy(a, b),
                      waits=cvw, inc="s_gpc")
            conv_last[0] = dma("gpsimd", dst_aps[i], bsl,
                               waits=[("s_gpc", cv)])
            cv_prev[buf] = conv_last[0]
        return conv_last[0]

    XL = [R1, R1 + 2048]
    XT32 = R1 + 4096
    WQA = [LN1, LN1 + 2048]
    WQBF = [YW, YW + 2048]
    KVE = LN1T

    e_tr = {}
    e_ev = {}
    e_cast = {}
    e_xspill = {}
    e_te_half = 0
    e_sp_q = 0
    e_sp_kv = 0
    e_sp_x = 0
    stage_bank_ev = {}
    stage_spill_ev = {}
    slab_f32_user = {0: 0, 1: 0}
    wq_last_mm = {}
    nsq_user = {}
    nm_ev = [0, 0]
    pend_norm = [None]
    norm_i = [0]
    NSQ = 47104

    def emit_norm(sq, ncol, nbuf):
        nm_ev[0] = emit("tensor",
                        lambda e, nbuf=nbuf:
                        e.matmul(ps[0:1, 2048:2560], ones_cb[:],
                                 bfbuf(nbuf, 512), start=True, stop=True),
                        waits=[("s_act", sq), ("s_dve", nm_ev[1])],
                        inc="s_te")
        rr = emit("vector",
                  lambda e: e.reduce_max(nmrow[0:1, 32:33],
                                         ps[0:1, 2048:2560], axis=AX.X),
                  waits=[("s_te", nm_ev[0])], inc="s_dve")
        nm_ev[1] = emit("vector",
                        lambda e, ncol=ncol:
                        e.tensor_max(nmrow[0:1, ncol:ncol + 1],
                                     nmrow[0:1, ncol:ncol + 1],
                                     nmrow[0:1, 32:33]),
                        waits=[("s_dve", rr)], inc="s_dve")

    for hf in range(2):
        for tt in range(16):
            gt = hf * 16 + tt
            xb = XL[gt % 2]
            ldw = []
            if gt - 2 in e_tr:
                ldw.append(("s_te", e_tr[gt - 2]))
            ld = dma("sync", f32buf(xb, 2048),
                     x_h.ap()[gt * 128:(gt + 1) * 128, :], waits=ldw,
                     sem="s_xa")
            b0 = 0 if gt % 2 == 0 else 4
            wt0 = [("s_xa", ld), ("s_gpc", e_const)]
            prev_dve = max([v for k, v in stage_bank_ev.items()
                            if k != "vmm"] + [nm_ev[1]], default=0)
            if prev_dve:
                wt0.append(("s_dve", prev_dve))
            te = None
            for c in range(16):
                te = emit("tensor",
                          lambda e, c=c, xb=xb, b0=b0:
                          e.transpose(ps[:, b0 * 512 + c * 128:
                                         b0 * 512 + c * 128 + 128],
                                      f32buf(xb + c * 128, 128), identf[:]),
                          waits=wt0 if c == 0 else [],
                          inc="s_te" if c == 15 else None)
            e_tr[gt] = te
            evw = [("s_te", te)]
            if gt - 1 in e_cast:
                evw.append(("s_act", e_cast[gt - 1]))
            if gt - 1 in e_xspill:
                evw.append(("s_xp", e_xspill[gt - 1]))
            ev = emit("vector",
                      lambda e, b0=b0: e.tensor_copy(
                          f32buf(XT32, 2048),
                          ps[:, b0 * 512:b0 * 512 + 2048]),
                      waits=evw, inc="s_dve")
            e_ev[gt] = ev
            stage_bank_ev[b0] = ev
            cw = [("s_dve", ev)]
            if e_te_half:
                cw.append(("s_te", e_te_half))
            cs = emit("scalar",
                      lambda e, tt=tt: e.activation(
                          abf[:, ZONE:ZONE + 32768].rearrange(
                              "p (c t) -> p c t", c=16)[:, :,
                                                        tt * 128:tt * 128
                                                        + 128],
                          f32buf(XT32, 2048).rearrange(
                              "p (c t) -> p c t", c=16),
                          AF.Copy, bias=0.0, scale=1.0),
                      waits=cw, inc="s_act")
            e_cast[gt] = cs
            if tt >= 8:
                own = hf * 8 + (tt - 8)
                e_xspill[gt] = dma("scalar", xts_h.ap()[own],
                                   f32buf(XT32, 2048),
                                   waits=[("s_dve", ev)], sem="s_xp")
                e_sp_x = e_xspill[gt]

        wt_x = [("s_act", e_cast[hf * 16 + 15])]

        for oc in range(32):
            isq = oc < 16
            h = oc % 16
            sl = oc % 2
            ldw = []
            if slab_f32_user[sl]:
                ldw.append(("s_act", slab_f32_user[sl]))
            ld = dma("sync", f32buf(WQA[sl], 2048), wqk_h.ap()[oc],
                     waits=ldw)
            wcw = [("s_in", ld)]
            if wq_last_mm.get(sl, 0):
                wcw.append(("s_te", wq_last_mm[sl]))
            if stage_bank_ev.get("vmm", 0):
                wcw.append(("s_te", stage_bank_ev["vmm"]))
            if oc == 0:
                wcw += wt_x
            wc = emit("scalar",
                      lambda e, sl=sl: e.activation(
                          bfbuf(WQBF[sl], 2048), f32buf(WQA[sl], 2048),
                          AF.Copy, bias=0.0, scale=1.0),
                      waits=wcw, inc="s_act")
            slab_f32_user[sl] = wc
            for tokb in range(4):
                pb = tokb % 2
                wt = [("s_act", wc)]
                bev = max(stage_bank_ev.get(("qk", pb), 0),
                          stage_bank_ev.get(0, 0))
                if bev:
                    wt.append(("s_dve", bev))
                te = None
                for c in range(16):
                    te = emit("tensor",
                              lambda e, c=c, sl=sl, tokb=tokb, pb=pb:
                              e.matmul(bank(pb),
                                       bfbuf(WQBF[sl] + c * 128, 128),
                                       abf[:, ZONE + c * 2048 + tokb * 512:
                                           ZONE + c * 2048 + tokb * 512
                                           + 512],
                                       start=(c == 0), stop=(c == 15)),
                              waits=wt if c == 0 else [],
                              inc="s_te" if c == 15 else None)
                wq_last_mm[sl] = te
                kve = KVE + pb * 512
                evw = [("s_te", te)]
                if stage_spill_ev.get(pb, 0):
                    evw.append(("s_scd", stage_spill_ev[pb]))
                if nsq_user.get(pb, 0):
                    evw.append(("s_act", nsq_user[pb]))
                if isq:
                    ev = emit("vector",
                              lambda e, kve=kve, pb=pb, oc=oc:
                              e.tensor_scalar(bfbuf(kve, 512), bank(pb),
                                              bq_s[:, oc:oc + 1],
                                              1.0 / 128.0,
                                              ALU.add, ALU.mult),
                              waits=evw, inc="s_dve")
                else:
                    ev = emit("vector",
                              lambda e, kve=kve, pb=pb, oc=oc:
                              e.tensor_scalar_add(bfbuf(kve, 512), bank(pb),
                                                  bq_s[:, oc:oc + 1]),
                              waits=evw, inc="s_dve")
                stage_bank_ev[("qk", pb)] = ev
                if isq:
                    if tokb >= 2:
                        slot = hf * 2 + (tokb - 2)
                        sp = dma("scalar",
                                 qts_h.ap()[h][:, slot * 512:slot * 512
                                               + 512],
                                 bfbuf(kve, 512), waits=[("s_dve", ev)])
                        stage_spill_ev[pb] = sp
                        e_sp_q = sp
                        do_norm = True
                    else:
                        do_norm = False
                else:
                    pos = hf * 4 + tokb
                    sp = dma("scalar",
                             kts_h.ap()[h][:, pos * 512:pos * 512 + 512],
                             bfbuf(kve, 512), waits=[("s_dve", ev)])
                    stage_spill_ev[pb] = sp
                    e_sp_kv = sp
                    do_norm = True
                if do_norm:
                    ncol = h if not isq else 16 + h
                    nbuf = NSQ + (norm_i[0] % 2) * 512
                    norm_i[0] += 1
                    sq = emit("scalar",
                              lambda e, kve=kve, nbuf=nbuf: e.activation(
                                  bfbuf(nbuf, 512), bfbuf(kve, 512),
                                  AF.Square, bias=0.0, scale=1.0),
                              waits=[("s_dve", ev), ("s_te", nm_ev[0])],
                              inc="s_act")
                    nsq_user[pb] = sq
                    if pend_norm[0] is not None:
                        emit_norm(*pend_norm[0])
                    pend_norm[0] = (sq, ncol, nbuf)

        for og in range(4):
            wv_cast = {}
            for c in range(16):
                sl = c % 2
                ldw = []
                if slab_f32_user[sl]:
                    ldw.append(("s_act", slab_f32_user[sl]))
                ld = dma("sync", f32buf(WQA[sl], 512), wv_h.ap()[og][c],
                         waits=ldw, sem="s_wv")
                wcw = [("s_wv", ld)]
                if stage_bank_ev.get("vmm", 0):
                    wcw.append(("s_te", stage_bank_ev["vmm"]))
                for mmv in wq_last_mm.values():
                    wcw.append(("s_te", mmv))
                wv_cast[c] = emit("scalar",
                                  lambda e, c=c, sl=sl: e.activation(
                                      bfbuf(YW + c * 512, 512),
                                      f32buf(WQA[sl], 512), AF.Copy,
                                      bias=0.0, scale=1.0),
                                  waits=wcw, inc="s_act")
                slab_f32_user[sl] = wv_cast[c]
            for tt in range(16):
                gt = hf * 16 + tt
                pb = 2 + tt % 2
                wt = [("s_act", wv_cast[15])]
                bev = max(stage_bank_ev.get(("v", pb), 0),
                          stage_bank_ev.get(0, 0))
                if bev:
                    wt.append(("s_dve", bev))
                te = None
                for c in range(16):
                    te = emit("tensor",
                              lambda e, c=c, tt=tt, pb=pb:
                              e.matmul(bank(pb),
                                       abf[:, ZONE + c * 2048 + tt * 128:
                                           ZONE + c * 2048 + tt * 128
                                           + 128],
                                       bfbuf(YW + c * 512, 512),
                                       start=(c == 0), stop=(c == 15)),
                              waits=wt if c == 0 else [],
                              inc="s_te" if c == 15 else None)
                stage_bank_ev["vmm"] = te
                e_te_half = te
                kve = KVE + 1024 + (tt % 2) * 512
                evw = [("s_te", te)]
                if stage_spill_ev.get(("v", tt % 2), 0):
                    evw.append(("s_scd", stage_spill_ev[("v", tt % 2)]))
                ev = emit("vector",
                          lambda e, kve=kve, pb=pb:
                          e.tensor_copy(bfbuf(kve, 512), bank(pb)),
                          waits=[w for w in evw], inc="s_dve")
                stage_bank_ev[("v", pb)] = ev
                sp = dma("scalar",
                         bass.AP(vs_h, og * 4 * T * 128 + gt * 128 * 128,
                                 [[128, 128], [T * 128, 4], [1, 128]]),
                         bfbuf(kve, 512).rearrange("p (h d) -> p h d", h=4),
                         waits=[("s_dve", ev)])
                stage_spill_ev[("v", tt % 2)] = sp
                e_sp_kv = sp

    if DEBUG:
        dbw = [("s_scd", e_sp_q), ("s_scd", e_sp_kv)]
        for hh in range(16):
            dma("gpsimd", dbg["qts"].ap()[hh], qts_h.ap()[hh], waits=dbw)
            dma("gpsimd", dbg["kts"].ap()[hh], kts_h.ap()[hh])
            dma("gpsimd", dbg["vs"].ap()[hh], vs_h.ap()[hh])

    if pend_norm[0] is not None:
        emit_norm(*pend_norm[0])
        pend_norm[0] = None

    e_conv_wo = conv_stream([wo_h.ap()[i] for i in range(16)],
                            [wob_h.ap()[i] for i in range(16)])
    e_conv_wg = conv_stream([wg_h.ap()[i] for i in range(64)],
                            [wgb_h.ap()[i] for i in range(64)],
                            gate=("s_te", e_te_half))
    e_conv_wu = conv_stream([wu_h.ap()[i] for i in range(64)],
                            [wub_h.ap()[i] for i in range(64)])
    e_conv_all = conv_stream(
        [wd_h.ap()[i // 4][:, (i % 4) * 2048:(i % 4) * 2048 + 2048]
         for i in range(64)],
        [wdb_h.ap()[i // 4][:, (i % 4) * 2048:(i % 4) * 2048 + 2048]
         for i in range(64)])

    e_nm1 = emit("vector",
                 lambda e: e.tensor_mul(nmrow[0:1, 32:48], nmrow[0:1, 0:16],
                                        nmrow[0:1, 16:32]),
                 waits=[("s_dve", nm_ev[1])], inc="s_dve")
    e_nm2 = emit("scalar",
                 lambda e: e.activation(nmrow[0:1, 48:64], nmrow[0:1, 32:48],
                                        AF.Sqrt, bias=0.0, scale=1.0),
                 waits=[("s_dve", e_nm1)], inc="s_act")
    e_nm3 = emit("vector",
                 lambda e: e.tensor_scalar_mul(nmrow[0:1, 32:48],
                                               nmrow[0:1, 48:64], -1.0),
                 waits=[("s_act", e_nm2)], inc="s_dve")
    e_nm4 = emit("tensor",
                 lambda e: e.matmul(ps[:, 2048:2064], ones_f[:],
                                    nmrow[0:1, 32:48], start=True, stop=True),
                 waits=[("s_dve", e_nm3), ("s_dve", nm_ev[1])], inc="s_te")
    e_negm = emit("vector",
                  lambda e: e.tensor_copy(negm_s[:], ps[:, 2048:2064]),
                  waits=[("s_te", e_nm4)], inc="s_dve")

    e_bo2mm = 0
    for dmc in range(16):
        ld = dma("sync", bfbuf(ZONE + ZWO, 2048), wob_h.ap()[dmc],
                 waits=[("s_gpd", e_conv_wo), ("s_te", e_te_half),
                        ("s_te", e_bo2mm)], sem="s_wo")
        for yc in range(16):
            ev = emit("tensor",
                      lambda e, yc=yc, dmc=dmc:
                      e.matmul(ps[:, 1024 + dmc:1024 + dmc + 1],
                               bfbuf(ZONE + ZWO + yc * 128, 128),
                               bv_bf[:, yc:yc + 1],
                               start=(yc == 0), stop=(yc == 15)),
                      waits=[("s_wo", ld), ("s_dve", e_bv)]
                      if yc == 0 else [],
                      inc="s_te" if yc == 15 else None)
            if ev is not None:
                e_bo2mm = ev
    e_bo2 = emit("vector",
                 lambda e: e.tensor_add(bo2_s[:], ps[:, 1024:1040], bo_s[:]),
                 waits=[("s_te", e_bo2mm)], inc="s_dve")

    e_out_last = 0
    zone_free_at = [("s_te", e_bo2mm)]
    prev_gu = [0]
    prev_dn = [0]
    prev_ln2 = [e_bo2]
    prev_exp_last = 0

    for w in range(NW):
        pref = PREF[w]
        nkt = pref * 4
        kw = pref * 512

        e_yev = {}
        e_attn = 0
        for h in range(H):
            ab = h % 2
            zkt = ZONE + ZKT + ab * 4096
            zv = ZONE + ZV + ab * 4096
            zqt = ZONE + ZQT + ab * 512
            if h < 2:
                ldqw = (list(zone_free_at) + [("s_scd", e_sp_kv),
                                              ("s_scd", e_sp_q),
                                              ("s_xp", e_sp_x)])
            else:
                ldqw = [("s_dve", e_yev[h - 2])]
            ldq = dma("sync", bfbuf(zqt, 512),
                      qts_h.ap()[h][:, w * 512:w * 512 + 512], waits=ldqw,
                      sem="s_qa")
            ldk = dma("sync", bfbuf(zkt, kw), kts_h.ap()[h][:, 0:kw],
                      sem="s_ka")
            ldv = dma("sync",
                      bfbuf(zv, nkt * 128).rearrange(
                          "p (t d) -> p t d", d=128),
                      bass.AP(vs_h, h * T * 128,
                              [[128, 128], [128 * 128, nkt], [1, 128]]),
                      sem="s_va")
            cmw = [("s_dve", e_negm), ("s_in", c_bias)]
            if prev_exp_last:
                cmw.append(("s_act", prev_exp_last))
            nmaskb = 2 if w < 2 else 4
            e_bias = 0
            for i in range(nmaskb):
                e_bias = emit("vector",
                              lambda e, i=i, w=w, h=h:
                              e.tensor_add(bias_s[:, 1 + i:2 + i],
                                           negm_s[:, h:h + 1],
                                           mb_s[:, w * 4 + i:w * 4 + i + 1]),
                              waits=cmw if i == 0 else [], inc="s_dve")

            STB = [4, 5]
            PTD = 4
            pv_ev = 0
            exp_ev = {}
            pt_guard = {}
            pend = None

            def emit_pv(kt, ptb, pt_ready):
                nonlocal pv_ev
                pvw = [("s_act", exp_ev[kt]), ("s_va", ldv)]
                if pt_ready is not None:
                    pvw.append(("s_dve", pt_ready))
                if kt == 0 and h > 0:
                    pvw.append(("s_dve", e_yev[h - 1]))
                pv_ev = emit("tensor",
                             lambda e, kt=kt, ptb=ptb, nkt=nkt, zv=zv:
                             e.matmul(bank(6),
                                      bfbuf(zv + kt * 128, 128),
                                      bfbuf(ptb, 512),
                                      start=(kt == 0),
                                      stop=(kt == nkt - 1)),
                             waits=pvw, inc="s_te")
                pt_guard[kt] = emit(
                    "tensor",
                    lambda e, kt=kt, ptb=ptb, nkt=nkt:
                    e.matmul(ps[0:1, 3584:4096], ones_cb[:],
                             bfbuf(ptb, 512),
                             start=(kt == 0), stop=(kt == nkt - 1)),
                    inc="s_te")

            for kt in range(nkt):
                blk = kt // 4
                sb = STB[kt % 2]
                wt = [("s_dve", e_bias), ("s_ka", ldk), ("s_qa", ldq)]
                if kt - 2 in exp_ev:
                    wt.append(("s_act", exp_ev[kt - 2]))
                elif prev_exp_last:
                    wt.append(("s_act", prev_exp_last))
                stt = emit("tensor",
                           lambda e, kt=kt, sb=sb, zkt=zkt, zqt=zqt:
                           e.matmul(bank(sb),
                                    bfbuf(zkt + kt * 128, 128),
                                    bfbuf(zqt, 512),
                                    start=True, stop=True),
                           waits=wt, inc="s_te")
                dve_ev = None
                if blk == pref - 1:
                    qs = kt % 4
                    dve_ev = emit("vector",
                                  lambda e, sb=sb, qs=qs:
                                  e.tensor_add(
                                      ps[:, sb * 512 + qs * 128:
                                         sb * 512 + qs * 128 + 128],
                                      ps[:, sb * 512 + qs * 128:
                                         sb * 512 + qs * 128 + 128],
                                      masktri[:]),
                                  waits=[("s_te", stt)], inc="s_dve")
                if blk in MASKPOS and blk < pref - 1:
                    bias_ap = bias_s[:, 1 + MASKPOS.index(blk):
                                     2 + MASKPOS.index(blk)]
                else:
                    bias_ap = negm_s[:, h:h + 1]
                ptb = ZONE + ZPT + (kt % PTD) * 512
                ew = [("s_te", stt)]
                if dve_ev:
                    ew.append(("s_dve", dve_ev))
                if kt - PTD in pt_guard:
                    ew.append(("s_te", pt_guard[kt - PTD]))
                exp_ev[kt] = emit("scalar",
                                  lambda e, sb=sb, ptb=ptb, b=bias_ap:
                                  e.activation(bfbuf(ptb, 512), bank(sb),
                                               AF.Exp, bias=b, scale=1.0),
                                  waits=ew, inc="s_act")
                pt_ready = None
                if blk == pref - 1:
                    ktr = kt % 4
                    for qs in range(4):
                        if ktr > qs:
                            pt_ready = emit(
                                "vector",
                                lambda e, ptb=ptb, qs=qs:
                                e.memset(bfbuf(ptb + qs * 128, 128), 0.0),
                                waits=[("s_act", exp_ev[kt])], inc="s_dve")
                if pend is not None:
                    emit_pv(*pend)
                pend = (kt, ptb, pt_ready)
            emit_pv(*pend)
            el = emit("vector",
                      lambda e: e.tensor_copy(l_sb[:], ps[0:1, 3584:4096]),
                      waits=[("s_te", pt_guard[nkt - 1])], inc="s_dve")
            er = emit("vector", lambda e: e.reciprocal(rl_sb[:], l_sb[:]),
                      waits=[("s_dve", el)], inc="s_dve")
            eb = emit("tensor",
                      lambda e: e.matmul(ps[:, 0:512], ones_f[:], rl_sb[:],
                                         start=True, stop=True),
                      waits=[("s_dve", er)], inc="s_te")
            ec = emit("vector",
                      lambda e: e.tensor_copy(rlbc_sb[:], ps[:, 0:512]),
                      waits=[("s_te", eb)], inc="s_dve")
            e_yev[h] = emit("vector",
                            lambda e, h=h:
                            e.tensor_mul(bfbuf(YW + h * 512, 512), bank(6),
                                         rlbc_sb[:]),
                            waits=[("s_te", pv_ev), ("s_dve", ec)],
                            inc="s_dve")
            prev_exp_last = exp_ev[nkt - 1]
            if DEBUG:
                dma("gpsimd", dbg["bias"].ap()[w][h], bias_s[:],
                    waits=[("s_dve", e_bias)])
                dma("gpsimd", dbg["l"].ap()[w][h:h + 1, :], l_sb[:],
                    waits=[("s_dve", e_yev[h])])
        e_attn = e_yev[H - 1]
        if DEBUG:
            dma("gpsimd", dbg["y"].ap()[w], bfbuf(YW, 8192),
                waits=[("s_dve", e_attn)])

        e_r1 = {}
        op_mm = {}
        for dmc in range(16):
            wsl = ZONE + ZWO + (dmc % 2) * 2048
            ldw = list(zone_free_at)
            if dmc - 2 in op_mm:
                ldw = [("s_te", op_mm[dmc - 2])]
            ldwv = dma("sync", bfbuf(wsl, 2048), wob_h.ap()[dmc],
                       waits=ldw, sem="s_wo")
            ldxw = [("s_xp", e_sp_x)]
            if dmc - 1 in e_r1:
                ldxw.append(("s_dve", e_r1[dmc - 1]))
            ldx = dma("sync",
                      f32buf(XTS, 512).rearrange("p (a b) -> p a b", a=4),
                      bass.AP(xts_h, w * 4 * 128 * 2048 + dmc * 128,
                              [[2048, 128], [128 * 2048, 4], [1, 128]]),
                      waits=ldxw, sem="s_xs")
            pb = dmc % 2
            wt0 = [("s_wo", ldwv), ("s_dve", e_attn)]
            if dmc - 2 in e_r1:
                wt0.append(("s_dve", e_r1[dmc - 2]))
            for yc in range(16):
                ev = emit("tensor",
                          lambda e, yc=yc, pb=pb, wsl=wsl:
                          e.matmul(bank(pb),
                                   bfbuf(wsl + yc * 128, 128),
                                   bfbuf(YW + yc * 512, 512),
                                   start=(yc == 0), stop=(yc == 15)),
                          waits=wt0 if yc == 0 else [],
                          inc="s_te" if yc == 15 else None)
                if ev is not None:
                    op_mm[dmc] = ev
            r1w = [("s_te", op_mm[dmc]), ("s_xs", ldx), ("s_dve", e_bo2)]
            if dmc < 4:
                r1w.append(("s_gpd", e_out_last))
            e_r1[dmc] = emit(
                "vector",
                lambda e, dmc=dmc, pb=pb:
                e.scalar_tensor_tensor(f32buf(R1 + dmc * 512, 512), bank(pb),
                                       bo2_s[:, dmc:dmc + 1],
                                       f32buf(XTS, 512), ALU.add, ALU.add),
                waits=r1w, inc="s_dve")
        if DEBUG:
            dma("gpsimd", dbg["r1"].ap()[w], f32buf(R1, 8192),
                waits=[("s_dve", e_r1[15])])

        def layer_norm(src_c0, dst_c0, dstbf_c0, wait_ev):
            mm_s = mm_q = 0
            for dmc in range(16):
                c1w = [("s_dve", wait_ev[dmc])]
                if mm_s:
                    c1w.append(("s_te", mm_s))
                c1 = emit("scalar",
                          lambda e, dmc=dmc:
                          e.activation(bfbuf(R1BF, 512),
                                       f32buf(src_c0 + dmc * 512, 512),
                                       AF.Copy, bias=0.0, scale=1.0),
                          waits=c1w, inc="s_act")
                c2w = []
                if mm_q:
                    c2w.append(("s_te", mm_q))
                c2 = emit("scalar",
                          lambda e, dmc=dmc:
                          e.activation(bfbuf(SQBF, 512),
                                       f32buf(src_c0 + dmc * 512, 512),
                                       AF.Square, bias=0.0, scale=1.0),
                          waits=c2w, inc="s_act")
                mm_s = emit("tensor",
                            lambda e, dmc=dmc:
                            e.matmul(ps[0:1, 1024:1536], ones_cb[:],
                                     bfbuf(R1BF, 512),
                                     start=(dmc == 0), stop=(dmc == 15)),
                            waits=[("s_act", c1)], inc="s_te")
                mm_q = emit("tensor",
                            lambda e, dmc=dmc:
                            e.matmul(ps[0:1, 1536:2048], ones_cb[:],
                                     bfbuf(SQBF, 512),
                                     start=(dmc == 0), stop=(dmc == 15)),
                            waits=[("s_act", c2)], inc="s_te")
            v1 = emit("vector",
                      lambda e: e.tensor_scalar_mul(srow[:],
                                                    ps[0:1, 1024:1536],
                                                    1.0 / 2048.0),
                      waits=[("s_te", mm_s), ("s_te", mm_q)], inc="s_dve")
            emit("vector",
                 lambda e: e.tensor_scalar_mul(qrow[:], ps[0:1, 1536:2048],
                                               1.0 / 2048.0),
                 inc="s_dve")
            v2 = emit("vector",
                      lambda e: e.tensor_mul(arow[:], srow[:], srow[:]),
                      waits=[("s_dve", v1)], inc="s_dve")
            ev = emit("vector",
                      lambda e: e.tensor_sub(qrow[:], qrow[:], arow[:]),
                      waits=[("s_dve", v2)], inc="s_dve")
            ev = emit("scalar",
                      lambda e: e.activation(arow[:], qrow[:], AF.Sqrt,
                                             bias=eps_s[:], scale=1.0),
                      waits=[("s_dve", ev)], inc="s_act")
            v3 = emit("vector", lambda e: e.reciprocal(arow[:], arow[:]),
                      waits=[("s_act", ev)], inc="s_dve")
            ev = emit("vector",
                      lambda e: e.tensor_mul(brow[:], srow[:], arow[:]),
                      waits=[("s_dve", v3)], inc="s_dve")
            emit("tensor",
                 lambda e: e.matmul(ps[:, 1024:1536], ones_f[:], arow[:],
                                    start=True, stop=True),
                 waits=[("s_dve", ev)], inc="s_te")
            eB = emit("tensor",
                      lambda e: e.matmul(ps[:, 1536:2048], ones_f[:],
                                         brow[:], start=True, stop=True),
                      inc="s_te")
            out_ev = {}
            for dmc in range(16):
                e1 = emit("vector",
                          lambda e, dmc=dmc:
                          e.tensor_mul(f32buf(SILU, 512),
                                       f32buf(src_c0 + dmc * 512, 512),
                                       ps[:, 1024:1536]),
                          waits=[("s_te", eB)] if dmc == 0 else [],
                          inc="s_dve")
                e2 = emit("vector",
                          lambda e, dmc=dmc:
                          e.tensor_sub(f32buf(dst_c0 + dmc * 512, 512),
                                       f32buf(SILU, 512),
                                       ps[:, 1536:2048]),
                          waits=[("s_dve", e1), ("s_gpd", e_conv_all)],
                          inc="s_dve")
                if dstbf_c0 is not None:
                    e2 = emit("scalar",
                              lambda e, dmc=dmc:
                              e.activation(bfbuf(dstbf_c0 + dmc * 512, 512),
                                           f32buf(dst_c0 + dmc * 512, 512),
                                           AF.Copy, bias=0.0, scale=1.0),
                              waits=[("s_dve", e2),
                                     ("s_gpd", e_conv_all)], inc="s_act")
                out_ev[dmc] = e2
            return out_ev

        ln1_ev = layer_norm(R1, LN1, LN1T, e_r1)
        if DEBUG:
            dma("gpsimd", dbg["ln1"].ap()[w], f32buf(LN1, 8192),
                waits=[("s_act", ln1_ev[15])])

        gu_mm = {}
        last_sil = {}
        last_mul = {}
        for hc in range(64):
            gsl = WGB + (hc % 2) * 2048
            usl = WUB + (hc % 2) * 2048
            if hc - 2 in gu_mm:
                ldw = [("s_te", gu_mm[hc - 2])]
            else:
                ldw = [("s_te", prev_gu[0]), ("s_gpd", e_conv_wu)]
            lg = dma("sync", bfbuf(gsl, 2048), wgb_h.ap()[hc], waits=ldw,
                     sem="s_ff")
            lu = dma("sync", bfbuf(usl, 2048), wub_h.ap()[hc],
                     sem="s_fu")
            pb = (hc % 2) * 2
            wt0 = [("s_ff", lg), ("s_fu", lu),
                   ("s_act", ln1_ev[15])]
            if hc - 2 in last_sil:
                wt0.append(("s_act", last_sil[hc - 2]))
                wt0.append(("s_dve", last_mul[hc - 2]))
            for c in range(16):
                emit("tensor",
                     lambda e, c=c, pb=pb, gsl=gsl:
                     e.matmul(bank(pb), bfbuf(gsl + c * 128, 128),
                              bfbuf(LN1T + c * 512, 512),
                              start=(c == 0), stop=(c == 15)),
                     waits=wt0 if c == 0 else [])
            for c in range(16):
                ev = emit("tensor",
                          lambda e, c=c, pb=pb, usl=usl:
                          e.matmul(bank(pb + 1), bfbuf(usl + c * 128, 128),
                                   bfbuf(LN1T + c * 512, 512),
                                   start=(c == 0), stop=(c == 15)),
                          inc="s_te" if c == 15 else None)
                if ev is not None:
                    gu_mm[hc] = ev
            slw = [("s_te", gu_mm[hc])]
            if hc - 1 in last_mul:
                slw.append(("s_dve", last_mul[hc - 1]))
            sl = emit("scalar",
                      lambda e, pb=pb:
                      e.activation(f32buf(SILU, 512), bank(pb), AF.Silu,
                                   bias=0.0, scale=1.0),
                      waits=slw, inc="s_act")
            last_sil[hc] = sl
            last_mul[hc] = emit("vector",
                                lambda e, hc=hc, pb=pb:
                                e.tensor_mul(bfbuf(ZONE + hc * 512, 512),
                                             f32buf(SILU, 512),
                                             bank(pb + 1)),
                                waits=[("s_act", sl)], inc="s_dve")
        e_r2 = {}
        dn_mm = {}
        for dmc in range(16):
            pb = 6 + dmc % 2
            for q in range(4):
                qi = dmc * 4 + q
                dsl = WDB + (qi % 2) * 2048
                if qi - 2 in dn_mm:
                    ldw = [("s_te", dn_mm[qi - 2])]
                else:
                    ldw = [("s_te", prev_dn[0]), ("s_gpd", e_conv_all)]
                ldd = dma("gpsimd", bfbuf(dsl, 2048),
                          wdb_h.ap()[dmc][:, q * 2048:q * 2048 + 2048],
                          waits=ldw, sem="s_fd")
                wt0 = [("s_fd", ldd), ("s_dve", last_mul[63])]
                if q == 0 and dmc - 2 in e_r2:
                    wt0.append(("s_dve", e_r2[dmc - 2]))
                for hh in range(16):
                    hc = q * 16 + hh
                    ev = emit("tensor",
                              lambda e, hh=hh, hc=hc, pb=pb, dsl=dsl:
                              e.matmul(bank(pb), bfbuf(dsl + hh * 128, 128),
                                       bfbuf(ZONE + hc * 512, 512),
                                       start=(hc == 0), stop=(hc == 63)),
                              waits=wt0 if hh == 0 else [],
                              inc="s_te" if hh == 15 else None)
                    if ev is not None:
                        dn_mm[qi] = ev
            e_r2[dmc] = emit("vector",
                             lambda e, dmc=dmc, pb=pb:
                             e.tensor_add(f32buf(R1 + dmc * 512, 512),
                                          bank(pb),
                                          f32buf(LN1 + dmc * 512, 512)),
                             waits=[("s_te", dn_mm[dmc * 4 + 3])],
                             inc="s_dve")

        ln2_ev = layer_norm(R1, LN1, None, e_r2)

        e_osb = {}
        tte = 0
        for ts in range(4):
            for dmc in range(16):
                wt0 = [("s_dve", ln2_ev[15])]
                if dmc == 0 and ts - 1 in e_osb:
                    wt0.append(("s_dve", e_osb[ts - 1]))
                ev = emit("tensor",
                          lambda e, dmc=dmc, ts=ts:
                          e.transpose(ps[:, dmc * 128:dmc * 128 + 128],
                                      f32buf(LN1 + dmc * 512 + ts * 128,
                                             128),
                                      identf[:]),
                          waits=wt0, inc="s_te" if dmc == 15 else None)
                if ev is not None:
                    tte = ev
            e_osb[ts] = emit("vector",
                             lambda e: e.tensor_copy(f32buf(OSB, 2048),
                                                     ps[:, 0:2048]),
                             waits=[("s_te", tte), ("s_gpd", e_out_last)],
                             inc="s_dve")
            e_out_last = dma("gpsimd",
                             out_h.ap()[w * 512 + ts * 128:
                                        w * 512 + ts * 128 + 128, :],
                             f32buf(OSB, 2048),
                             waits=[("s_dve", e_osb[ts])])
        zone_free_at = [("s_te", dn_mm[63]), ("s_te", tte)]
        prev_gu[0] = gu_mm[63]
        prev_dn[0] = dn_mm[63]
        prev_ln2[0] = ln2_ev[15]

    emit("gpsimd", lambda e: None, waits=[("s_gpd", e_out_last)])

    with contextlib.ExitStack() as stack:
        sems = {s: stack.enter_context(nc.semaphore(s)) for s in sem_names}
        block = stack.enter_context(nc.Block())

        def make_body(eng_name):
            def body(e):
                for waits, fn, inc, inc_by in ops[eng_name]:
                    for s, v in waits:
                        e.wait_ge(sems[s], v)
                    inst = fn(e)
                    if inc is not None:
                        inst.then_inc(sems[inc], inc_by)
            return body

        block.sync(make_body("sync"))
        block.tensor(make_body("tensor"))
        block.vector(make_body("vector"))
        block.scalar(make_body("scalar"))
        block.gpsimd(make_body("gpsimd"))
    return nc


def _host_pack(W_qkv, w_gate, w_up, w_down, W_out):
    W = np.asarray(W_qkv, dtype=np.float32)
    wqk = np.empty((32, 128, 2048), dtype=np.float32)
    for oc in range(32):
        blk = W[:, oc * 128:(oc + 1) * 128]
        wqk[oc] = blk.reshape(16, 128, 128).transpose(1, 0, 2).reshape(
            128, 2048)
    wv = np.empty((4, 16, 128, 512), dtype=np.float32)
    for og in range(4):
        wv[og] = W[:, 4096 + og * 512:4096 + (og + 1) * 512].reshape(
            16, 128, 512)
    G = np.asarray(w_gate, dtype=np.float32).T
    U = np.asarray(w_up, dtype=np.float32).T
    Dn = np.asarray(w_down, dtype=np.float32).T
    Wo = np.asarray(W_out, dtype=np.float32)

    def pack(M, n_out):
        din = M.shape[0]
        ncin = din // 128
        out = np.empty((n_out, 128, ncin * 128), dtype=np.float32)
        for o in range(n_out):
            blk = M[:, o * 128:(o + 1) * 128]
            out[o] = blk.reshape(ncin, 128, 128).transpose(1, 0, 2).reshape(
                128, ncin * 128)
        return out

    return (wqk, wv, pack(G, 64), pack(U, 64), pack(Dn, 16), pack(Wo, 16))


PERM = {0: [2, 3, 0, 1, 4, 5, 6, 7],
        1: [0, 1, 2, 3, 6, 7, 4, 5]}
OWN = {0: [0, 1, 6, 7], 1: [2, 3, 4, 5]}
MB = {0: [[NEG, NEG, 0.0, 0.0], [NEG, NEG, 0.0, 0.0],
          [0.0, 0.0, 0.0, 0.0], [0.0, 0.0, 0.0, 0.0]],
      1: [[0.0, 0.0, 0.0, 0.0], [0.0, 0.0, 0.0, 0.0],
          [0.0, 0.0, NEG, NEG], [0.0, 0.0, NEG, NEG]]}

_cache = {}


def kernel(L0_attn_x, L0_attn_W_qkv, L0_attn_b_qkv, L0_attn_W_out,
           L0_attn_b_out, L0_ffn_w_gate, L0_ffn_w_up, L0_ffn_w_down):
    x = np.ascontiguousarray(np.asarray(L0_attn_x, dtype=np.float32))
    B = x.shape[0]
    wqk, wv, wg, wu, wd, wo = _host_pack(L0_attn_W_qkv, L0_ffn_w_gate,
                                         L0_ffn_w_up, L0_ffn_w_down,
                                         L0_attn_W_out)
    bq = np.ascontiguousarray(
        np.asarray(L0_attn_b_qkv, dtype=np.float32).reshape(48, 128).T)
    bo = np.ascontiguousarray(
        np.asarray(L0_attn_b_out, dtype=np.float32).reshape(16, 128).T)

    if "nc" not in _cache:
        _cache["nc"] = build_program()
    nc = _cache["nc"]

    in_maps = []
    for core in range(8):
        b, half = core // 2, core % 2
        xb = np.ascontiguousarray(
            x[b].reshape(8, 512, 2048)[PERM[half]].reshape(4096, 2048))
        mb = np.zeros((128, 16), dtype=np.float32)
        for s in range(4):
            for i in range(4):
                mb[:, s * 4 + i] = MB[half][s][i]
        in_maps.append({"x": xb, "wqk": wqk, "wv": wv, "wg": wg, "wu": wu,
                        "wd": wd, "wo": wo, "bq": bq, "bo": bo, "mb": mb})

    res = run_bass_kernel_spmd(nc, in_maps, core_ids=list(range(8)))

    out = np.empty((B, T, D), dtype=np.float32)
    for core in range(8):
        b, half = core // 2, core % 2
        shard = res.results[core]["out"]
        for s, g in enumerate(OWN[half]):
            out[b, g * 512:(g + 1) * 512] = shard[s * 512:(s + 1) * 512]
    if DEBUG:
        kernel.last_results = res
    return out
